# version 02
# baseline (speedup 1.0000x reference)
import numpy as np
import concourse.bass as bass
import concourse.mybir as mybir
from concourse.bass_utils import run_bass_kernel_spmd

F32 = mybir.dt.float32
BF16 = mybir.dt.float16
I32 = mybir.dt.int32
Alu = mybir.AluOpType
Act = mybir.ActivationFunctionType

D = 1024
E = 2048
NS = 16
DR = 64
L = 8192
FF = 4096
TL = 512
NT = L // TL
SL = 256
NSUB = TL // SL
DCH = D // 128
ECH = E // 128
FCH = FF // 128
LNEPS = 1e-5
QB = 2048.5

C_M8, C_M7_75, C_I256, C_M1, C_MHALF, C_EPS, C_ONE, C_256 = range(8)
CVALS = [-8.0, -7.75, 1.0 / 256, -1.0, -0.5, LNEPS, 1.0, 256.0]


class Plan:
    ENGS = ("sync", "vector", "scalar", "gpsimd", "tensor")

    def __init__(self, nc):
        self.nc = nc
        self.q = {e: [] for e in self.ENGS}
        self.sems = {}
        self.cnt = {}
        self.watermark = {e: {} for e in self.ENGS}
        self.buf = {}
        self.esem = {e: "s_" + e for e in self.ENGS}
        self.dsem = {"sync": "d_sync", "gpsimd": "d_gpsimd", "scalar": "d_scalar",
                     "vector": "d_vector"}
        for s in list(self.esem.values()) + list(self.dsem.values()):
            self.cnt[s] = 0

    def op(self, eng, fn, reads=(), writes=(), dma=False):
        sem_name = self.dsem[eng] if dma else self.esem[eng]
        inc = 16 if dma else 1
        waits = {}

        def need(ev):
            if ev is not None and waits.get(ev[0], 0) < ev[1]:
                waits[ev[0]] = ev[1]

        for b in reads:
            st = self.buf.get(b)
            if st is None:
                raise KeyError(f"read of never-written buffer {b!r}")
            need(st["w"])
        for b in writes:
            st = self.buf.get(b)
            if st is not None:
                need(st["w"])
                for ev in st["r"]:
                    need(ev)
        wm = self.watermark[eng]
        wl = [(s, v) for s, v in waits.items() if wm.get(s, 0) < v]
        for s, v in wl:
            wm[s] = v
        self.cnt[sem_name] += inc
        ev_new = (sem_name, self.cnt[sem_name])

        def emit(engine_obj, wl=tuple(wl), fn=fn, sem_name=sem_name, inc=inc):
            for s, v in wl:
                engine_obj.wait_ge(self.sems[s], v)
            fn(engine_obj, self.sems[sem_name], inc)

        self.q[eng].append(emit)
        for b in writes:
            self.buf[b] = {"w": ev_new, "r": []}
        for b in reads:
            self.buf[b]["r"].append(ev_new)
        return ev_new

    def barrier(self):
        pairs = [(s, v) for s, v in self.cnt.items() if v > 0]
        for eng in self.ENGS:
            wm = self.watermark[eng]
            wl = [(s, v) for s, v in pairs if wm.get(s, 0) < v]
            for s, v in wl:
                wm[s] = v

            def emit(engine_obj, wl=tuple(wl)):
                for s, v in wl:
                    engine_obj.wait_ge(self.sems[s], v)

            self.q[eng].append(emit)

    def final_waits(self, eng):
        pairs = [(s, v) for s, v in self.cnt.items() if v > 0]

        def emit(engine_obj, pairs=tuple(pairs)):
            for s, v in pairs:
                engine_obj.wait_ge(self.sems[s], v)

        self.q[eng].append(emit)


def dma(p, queue, dst, src, reads, writes):
    def fn(e, sem, inc):
        e.dma_start(out=dst, in_=src).then_inc(sem, inc)
    return p.op(queue, fn, reads=reads, writes=writes, dma=True)


def build():
    nc = bass.Bass("TRN2", target_bir_lowering=False)
    p = Plan(nc)

    xT = nc.dram_tensor("xT", [128, DCH * L], F32, kind="ExternalInput").ap()
    win_T = nc.dram_tensor("win_T", [D, 2 * E], F32, kind="ExternalInput").ap()
    wxp_T = nc.dram_tensor("wxp_T", [128, ECH * 96], F32, kind="ExternalInput").ap()
    wdt_T = nc.dram_tensor("wdt_T", [DR, E], F32, kind="ExternalInput").ap()
    wout_T = nc.dram_tensor("wout_T", [E, D], F32, kind="ExternalInput").ap()
    w1h = nc.dram_tensor("w1h", [D, FF], F32, kind="ExternalInput").ap()
    w2h = nc.dram_tensor("w2h", [FF, D], F32, kind="ExternalInput").ap()
    convw = nc.dram_tensor("convw", [128, ECH * 4], F32, kind="ExternalInput").ap()
    convb = nc.dram_tensor("convb", [128, ECH], F32, kind="ExternalInput").ap()
    dtbh = nc.dram_tensor("dtbh", [128, ECH], F32, kind="ExternalInput").ap()
    alog = nc.dram_tensor("alog", [128, ECH * NS], F32, kind="ExternalInput").ap()
    dpar = nc.dram_tensor("dpar", [128, ECH], F32, kind="ExternalInput").ap()
    b1h = nc.dram_tensor("b1h", [128, FCH], F32, kind="ExternalInput").ap()
    b2h = nc.dram_tensor("b2h", [128, DCH], F32, kind="ExternalInput").ap()
    outT = nc.dram_tensor("outT", [128, DCH * L], F32, kind="ExternalOutput").ap()
    wout_bf = nc.dram_tensor("wout_bf", [E, D], BF16, kind="Internal").ap()
    w1_bf = nc.dram_tensor("w1_bf", [D, FF], BF16, kind="Internal").ap()
    w2_bf = nc.dram_tensor("w2_bf", [FF, D], BF16, kind="Internal").ap()
    x1d = nc.dram_tensor("x1d", [128, DCH * L], BF16, kind="Internal").ap()

    ACOLS = 73792
    FCOLS = 6144

    from contextlib import ExitStack
    with ExitStack() as stack:
        def SB(name, shape, dt):
            return stack.enter_context(nc.sbuf_tensor(name, shape, dt))

        def PS(name):
            return stack.enter_context(nc.psum_tensor(name, [128, TL], F32))

        def SEM(name):
            return stack.enter_context(nc.semaphore(name))

        cpool = SB("cpool", [128, 8], F32)
        cw = SB("cw", [128, ECH * 4], F32)
        cb = SB("cb", [128, ECH], F32)
        dtb = SB("dtb", [128, ECH], F32)
        aneg = SB("aneg", [128, ECH * NS], F32)
        dpt = SB("dpt", [128, ECH], F32)
        bb1 = SB("bb1", [128, FCH], F32)
        bb2 = SB("bb2", [128, DCH], F32)
        wxp = SB("wxp", [128, ECH * 96], BF16)
        wdt = SB("wdt", [DR, E], BF16)
        ones = SB("ones", [128, 1], BF16)
        carry = SB("carry", [128, ECH * NS], F32)
        lrow = SB("lrow", [1, 2 * TL], F32)
        lrow2 = SB("lrow2", [1, TL], F32)
        rsb1 = SB("rsb1", [128, TL], F32)
        nmb1 = SB("nmb1", [128, TL], F32)
        rsb2 = SB("rsb2", [128, TL], F32)
        nmb2 = SB("nmb2", [128, TL], F32)
        dtbf = SB("dtbf", [DR, TL], BF16)
        xdbl = SB("xdbl", [96, TL], F32)
        wkp = SB("wkp", [128, 2 * D], BF16)
        dut = SB("dut", [128, SL], BF16)
        fi1 = SB("fi1", [128, TL], I32)
        fui = SB("fui", [128, TL], I32)
        ffi = SB("ffi", [128, TL], I32)
        f32a = SB("f32a", [128, FCOLS], F32)
        arena_t = SB("arena", [128, ACOLS], BF16)
        pm_l = [PS(f"pm{i}") for i in range(8)]
        s0, s1, s2, s3, s4 = [SEM(n) for n in ("s_sync", "s_vector", "s_scalar", "s_gpsimd", "s_tensor")]
        d0, d1, d2, d3 = [SEM(n) for n in ("d_sync", "d_gpsimd", "d_scalar", "d_vector")]
        block = stack.enter_context(nc.Block())
        p.sems = {"s_sync": s0, "s_vector": s1, "s_scalar": s2, "s_gpsimd": s3,
                  "s_tensor": s4, "d_sync": d0, "d_gpsimd": d1, "d_scalar": d2,
                  "d_vector": d3}
        pm = pm_l
        T = dict(cpool=cpool, cw=cw, cb=cb, dtb=dtb, aneg=aneg, dpt=dpt,
                 bb1=bb1, bb2=bb2, wxp=wxp, wdt=wdt, ones=ones, carry=carry,
                 lrow=lrow, lrow2=lrow2, rsb1=rsb1, nmb1=nmb1, rsb2=rsb2,
                 nmb2=nmb2, dtbf=dtbf, xdbl=xdbl, wkp=wkp, dut=dut, fi1=fi1,
                 fui=fui, ffi=ffi, f32a=f32a, arena=arena_t, pm=pm)
        H = dict(xT=xT, win_T=win_T, wxp_T=wxp_T, wdt_T=wdt_T, wout_T=wout_T,
                 w1h=w1h, w2h=w2h, convw=convw, convb=convb, dtbh=dtbh,
                 alog=alog, dpar=dpar, b1h=b1h, b2h=b2h, outT=outT,
                 wout_bf=wout_bf, w1_bf=w1_bf, w2_bf=w2_bf, x1d=x1d)
        plan(p, T, H, ACOLS)

        @block.sync
        def _(sync):
            for f in p.q["sync"]:
                f(sync)

        @block.vector
        def _(vector):
            for f in p.q["vector"]:
                f(vector)

        @block.scalar
        def _(scalar):
            for f in p.q["scalar"]:
                f(scalar)

        @block.gpsimd
        def _(gpsimd):
            for f in p.q["gpsimd"]:
                f(gpsimd)

        @block.tensor
        def _(tensor):
            for f in p.q["tensor"]:
                f(tensor)

    return nc


def plan(p, T, H, ACOLS):
    ar = T["arena"]
    fa = T["f32a"]
    pm = T["pm"]
    cp = T["cpool"]

    def arr(off, n):
        return ar[:, off:off + n]

    def cap(i):
        return cp[:, i:i + 1]

    def cap1(i):
        return cp[0:1, i:i + 1]

    def vop(fn, reads, writes):
        p.op("vector", fn, reads=reads, writes=writes)

    def sop(fn, reads, writes):
        p.op("scalar", fn, reads=reads, writes=writes)

    def mm_group(bank_ap, pairs, reads, writes):
        def fn(e, sem, inc, pairs=tuple(pairs)):
            n = len(pairs)
            for i, (lh, rh) in enumerate(pairs):
                mi = e.matmul(bank_ap, lh, rh, start=(i == 0), stop=(i == n - 1))
                if i == n - 1:
                    mi.then_inc(sem, inc)
        p.op("tensor", fn, reads=reads, writes=writes)

    f_sp1, f_sp2, f_del, f_yc, f_a32 = 0, TL, 2 * TL, 3 * TL, 4 * TL
    SP1 = fa[:, f_sp1:f_sp1 + TL]
    SP2 = fa[:, f_sp2:f_sp2 + TL]
    DEL = fa[:, f_del:f_del + TL]
    YC = fa[:, f_yc:f_yc + TL]

    o_win = 0
    o_x = o_win + DCH * 4096
    o_sq = o_x + DCH * TL
    o_xse = o_sq + DCH * TL
    o_res = o_xse + ECH * (TL + 4)
    o_at = o_res + ECH * TL
    o_xs2 = o_at + NS * SL
    o_BB = o_xs2 + NS * SL
    o_CB = o_BB + NS * SL
    p1_end = o_CB + NS * SL
    assert p1_end <= ACOLS, p1_end
    o_w1 = 0
    o_x1t = o_w1 + DCH * FF
    o_xn3 = o_x1t + DCH * TL
    o_hq = o_xn3 + DCH * TL
    o_sq2 = o_hq + FCH * TL
    o_ob = o_sq2 + DCH * TL
    p2_end = o_ob + DCH * TL
    assert p2_end <= ACOLS, p2_end

    def u_off(c):
        return o_xse + c * (TL + 4) + 4

    for i, v in enumerate(CVALS):
        p.op("vector", lambda e, s, ii, i=i, v=v: e.memset(cp[:, i:i + 1], v).then_inc(s, ii),
             writes=[f"cp{i}"])
    dma(p, "sync", T["cw"][:], H["convw"], [], ["cw"])
    dma(p, "sync", T["cb"][:], H["convb"], [], ["cb"])
    dma(p, "sync", T["dtb"][:], H["dtbh"], [], ["dtb"])
    dma(p, "sync", T["aneg"][:], H["alog"], [], ["aneg"])
    dma(p, "sync", T["dpt"][:], H["dpar"], [], ["dpt"])
    dma(p, "sync", T["bb1"][:], H["b1h"], [], ["bb1"])
    dma(p, "sync", T["bb2"][:], H["b2h"], [], ["bb2"])
    dma(p, "gpsimd", T["wxp"][:], H["wxp_T"], [], ["wxp"])
    dma(p, "gpsimd", T["wdt"][:], H["wdt_T"], [], ["wdt"])
    sop(lambda e, s, i: e.activation(T["aneg"][:], T["aneg"][:], Act.Exp).then_inc(s, i),
        ["aneg"], ["aneg"])
    vop(lambda e, s, i: e.tensor_scalar_mul(T["aneg"][:], T["aneg"][:], -1.0).then_inc(s, i),
        ["aneg"], ["aneg"])
    vop(lambda e, s, i: e.tensor_scalar(T["bb1"][:], T["bb1"][:], 256.0, QB, Alu.mult, Alu.add).then_inc(s, i),
        ["bb1"], ["bb1"])
    vop(lambda e, s, i: e.tensor_scalar(T["bb2"][:], T["bb2"][:], 256.0, QB, Alu.mult, Alu.add).then_inc(s, i),
        ["bb2"], ["bb2"])
    vop(lambda e, s, i: e.memset(T["ones"][:], 1.0).then_inc(s, i), [], ["ones"])
    vop(lambda e, s, i: e.memset(T["carry"][:], 0.0).then_inc(s, i), [], ["carry"])
    vop(lambda e, s, i: e.memset(
        ar[:, o_xse:o_xse + ECH * (TL + 4)].rearrange("p (c n) -> p c n", c=ECH)[:, :, 0:4],
        0.0).then_inc(s, i), [], [f"hist_{c}" for c in range(ECH)])

    FSTG = fa[:, 0:2048]
    for k in range(DCH):
        for h in range(2):
            dma(p, "sync", FSTG, H["win_T"][k * 128:(k + 1) * 128, h * 2048:(h + 1) * 2048],
                [], ["FSTG"])
            vop(lambda e, s, i, k=k, h=h: e.tensor_copy(
                arr(o_win + k * 4096 + h * 2048, 2048), FSTG).then_inc(s, i),
                ["FSTG"], [f"win_{k}", "FSTG"])
    win_keys = [f"win_{k}" for k in range(DCH)]

    BSTG = arr(o_at, 2048)

    def cast_dram(srcap, dstap, rows, cols, tag):
        w = min(cols, 2048)
        for k in range(rows // 128):
            for h in range(cols // w):
                dma(p, "sync", FSTG[:, :w],
                    srcap[k * 128:(k + 1) * 128, h * w:(h + 1) * w],
                    [], ["FSTG"])
                vop(lambda e, s, i, w=w: e.tensor_copy(
                    BSTG[:, :w], FSTG[:, :w]).then_inc(s, i),
                    ["FSTG"], ["BSTG", "FSTG"])
                dma(p, "gpsimd",
                    dstap[k * 128:(k + 1) * 128, h * w:(h + 1) * w],
                    BSTG[:, :w], ["BSTG"], [f"wc_{tag}", "BSTG"])
    cast_dram(H["wout_T"], H["wout_bf"], E, D, "wo")
    cast_dram(H["w1h"], H["w1_bf"], D, FF, "w1")
    cast_dram(H["w2h"], H["w2_bf"], FF, D, "w2")

    p.barrier()

    def ln_stats(src_off, sq_off, src_keys, rsb, nmb, rs_key, nm_key):
        sop(lambda e, s, i: e.activation(arr(sq_off, DCH * TL), arr(src_off, DCH * TL), Act.Square).then_inc(s, i),
            src_keys, ["SQS"])
        pairs_x = [(T["ones"][:], arr(src_off + c * TL, TL)) for c in range(DCH)]
        pairs_q = [(T["ones"][:], arr(sq_off + c * TL, TL)) for c in range(DCH)]
        mm_group(pm[4][0:1, :], pairs_x, src_keys + ["ones"], ["BNK4"])
        mm_group(pm[5][0:1, :], pairs_q, ["SQS", "ones"], ["BNK5"])
        lr, lr2 = T["lrow"], T["lrow2"]
        vop(lambda e, s, i: e.tensor_scalar_mul(lr[0:1, 0:TL], pm[4][0:1, :], 1.0 / D).then_inc(s, i),
            ["BNK4"], ["LR0"])
        vop(lambda e, s, i: e.tensor_scalar_mul(lr[0:1, TL:2 * TL], pm[5][0:1, :], 1.0 / D).then_inc(s, i),
            ["BNK5"], ["LR1"])
        vop(lambda e, s, i: e.tensor_tensor(lr2[0:1, 0:TL], lr[0:1, 0:TL], lr[0:1, 0:TL], Alu.mult).then_inc(s, i),
            ["LR0"], ["LR2"])
        vop(lambda e, s, i: e.tensor_sub(lr[0:1, TL:2 * TL], lr[0:1, TL:2 * TL], lr2[0:1, 0:TL]).then_inc(s, i),
            ["LR1", "LR2"], ["LR1"])
        sop(lambda e, s, i: e.activation(lr[0:1, TL:2 * TL], lr[0:1, TL:2 * TL], Act.Ln, bias=cap1(C_EPS)).then_inc(s, i),
            ["LR1", "cp5"], ["LR1"])
        sop(lambda e, s, i: e.activation(lr[0:1, TL:2 * TL], lr[0:1, TL:2 * TL], Act.Exp, scale=cap1(C_MHALF)).then_inc(s, i),
            ["LR1", "cp4"], ["LR1"])
        vop(lambda e, s, i: e.tensor_tensor(lr[0:1, 0:TL], lr[0:1, 0:TL], lr[0:1, TL:2 * TL], Alu.mult).then_inc(s, i),
            ["LR0", "LR1"], ["LR0"])
        vop(lambda e, s, i: e.tensor_scalar_mul(lr[0:1, 0:TL], lr[0:1, 0:TL], -1.0).then_inc(s, i),
            ["LR0"], ["LR0"])
        dma(p, "scalar", rsb[:],
            lr[0:1, TL:2 * TL].rearrange("o (u n) -> o u n", u=1).broadcast_to([1, 128, TL]),
            ["LR1"], [rs_key])
        dma(p, "scalar", nmb[:],
            lr[0:1, 0:TL].rearrange("o (u n) -> o u n", u=1).broadcast_to([1, 128, TL]),
            ["LR0"], [nm_key])

    for ti in range(NT):
        t0 = ti * TL
        srcx = H["xT"].rearrange("p (c l) -> p c l", c=DCH)[:, :, t0:t0 + TL]
        dma(p, "gpsimd", arr(o_x, DCH * TL), srcx, [], ["x"])
        ln_stats(o_x, o_sq, ["x"], T["rsb1"], T["nmb1"], "rs1", "nm1")
        for c in range(DCH):
            vop(lambda e, s, i, c=c: e.tensor_tensor(arr(o_sq + c * TL, TL), arr(o_x + c * TL, TL), T["rsb1"][:], Alu.mult).then_inc(s, i),
                ["x", "rs1"], [f"xn_{c}", "SQS"])
            vop(lambda e, s, i, c=c: e.tensor_tensor(arr(o_sq + c * TL, TL), arr(o_sq + c * TL, TL), T["nmb1"][:], Alu.add).then_inc(s, i),
                [f"xn_{c}", "nm1"], [f"xn_{c}", "SQS"])
        xn_keys = [f"xn_{c}" for c in range(DCH)]

        for m in range(2 * ECH):
            bank = m % 4
            pairs = [(arr(o_win + k * 4096 + m * 128, 128), arr(o_sq + k * TL, TL))
                     for k in range(DCH)]
            mm_group(pm[bank][:, :], pairs, xn_keys + win_keys + ["SQS"], [f"BNK{bank}"])
            if m < ECH:
                dst, wkeys = arr(o_xse + m * (TL + 4) + 4, TL), [f"xs_{m}", f"u_{m}"]
            else:
                dst, wkeys = arr(o_res + (m - ECH) * TL, TL), [f"res_{m - ECH}", f"sr_{m - ECH}", f"yg_{m - ECH}"]
            vop(lambda e, s, i, dst=dst, bank=bank: e.tensor_copy(dst, pm[bank][:, :]).then_inc(s, i),
                [f"BNK{bank}"], wkeys)

        for c in range(ECH):
            base = o_xse + c * (TL + 4)
            vop(lambda e, s, i, base=base, c=c: e.tensor_scalar(
                SP1, arr(base + 1, TL), T["cw"][:, c * 4:c * 4 + 1], T["cb"][:, c:c + 1],
                Alu.mult, Alu.add).then_inc(s, i),
                [f"xs_{c}", f"hist_{c}", "cw", "cb"], ["FS1"])
            for kk in range(1, 4):
                vop(lambda e, s, i, base=base, c=c, kk=kk: e.scalar_tensor_tensor(
                    SP1, arr(base + 1 + kk, TL), T["cw"][:, c * 4 + kk:c * 4 + kk + 1], SP1,
                    Alu.mult, Alu.add).then_inc(s, i),
                    [f"xs_{c}", f"hist_{c}", "cw", "FS1"], ["FS1"])
            vop(lambda e, s, i, base=base: e.tensor_copy(arr(base, 4), arr(base + TL, 4)).then_inc(s, i),
                [f"xs_{c}", "FS1"], [f"hist_{c}"])
            sop(lambda e, s, i, base=base: e.activation(arr(base + 4, TL), SP1, Act.Silu).then_inc(s, i),
                ["FS1", f"hist_{c}"], [f"u_{c}", f"xs_{c}"])
            sop(lambda e, s, i, c=c: e.activation(arr(o_res + c * TL, TL), arr(o_res + c * TL, TL), Act.Silu).then_inc(s, i),
                [f"res_{c}"], [f"sr_{c}", f"res_{c}"])
        u_keys = [f"u_{c}" for c in range(ECH)]

        pairs = [(T["wxp"][:, k * 96:(k + 1) * 96], arr(u_off(k), TL)) for k in range(ECH)]
        mm_group(pm[6][0:96, :], pairs, u_keys + ["wxp"], ["BNK6"])
        vop(lambda e, s, i: e.tensor_copy(T["xdbl"][:, :], pm[6][0:96, :]).then_inc(s, i),
            ["BNK6"], ["xdbl"])
        vop(lambda e, s, i: e.tensor_copy(T["dtbf"][:, :], T["xdbl"][0:DR, :]).then_inc(s, i),
            ["xdbl"], ["dtbf"])

        for sb in range(NSUB):
            so = sb * SL
            for n in range(NS):
                dma(p, "gpsimd", arr(o_BB + n * SL, SL),
                    T["xdbl"][DR + n:DR + n + 1, so:so + SL].rearrange("o (u n) -> o u n", u=1).broadcast_to([1, 128, SL]),
                    ["xdbl"], [f"BB_{n}"])
                dma(p, "gpsimd", arr(o_CB + n * SL, SL),
                    T["xdbl"][DR + NS + n:DR + NS + n + 1, so:so + SL].rearrange("o (u n) -> o u n", u=1).broadcast_to([1, 128, SL]),
                    ["xdbl"], [f"CB_{n}"])
            bkeys = [f"BB_{n}" for n in range(NS)]
            ckeys = [f"CB_{n}" for n in range(NS)]
            for c in range(ECH):
                mm_group(pm[7][:, :SL], [(T["wdt"][:, c * 128:(c + 1) * 128], T["dtbf"][:, so:so + SL])],
                         ["dtbf", "wdt"], ["BNK7"])
                vop(lambda e, s, i, c=c: e.tensor_scalar(
                    SP1[:, :SL], pm[7][:, :SL], T["dtb"][:, c:c + 1], None, Alu.add).then_inc(s, i),
                    ["BNK7", "dtb"], ["FS1"])
                sop(lambda e, s, i: e.activation(SP2[:, :SL], SP1[:, :SL], Act.Abs).then_inc(s, i),
                    ["FS1"], ["FS2"])
                sop(lambda e, s, i: e.activation(SP2[:, :SL], SP2[:, :SL], Act.Exp, scale=cap(C_M1)).then_inc(s, i),
                    ["FS2", "cp3"], ["FS2"])
                sop(lambda e, s, i: e.activation(SP2[:, :SL], SP2[:, :SL], Act.Ln, bias=cap(C_ONE)).then_inc(s, i),
                    ["FS2", "cp6"], ["FS2"])
                vop(lambda e, s, i: e.tensor_scalar_max(SP1[:, :SL], SP1[:, :SL], 0.0).then_inc(s, i),
                    ["FS1"], ["FS1"])
                vop(lambda e, s, i: e.tensor_add(DEL[:, :SL], SP1[:, :SL], SP2[:, :SL]).then_inc(s, i),
                    ["FS1", "FS2"], ["FDL"])
                vop(lambda e, s, i, c=c, so=so: e.tensor_tensor(
                    T["dut"][:], DEL[:, :SL], arr(u_off(c) + so, SL),
                    Alu.mult).then_inc(s, i),
                    ["FDL", f"u_{c}"], ["DU"])
                for n in range(NS):
                    sop(lambda e, s, i, c=c, n=n: e.activation(
                        fa[:, f_a32 + n * SL:f_a32 + (n + 1) * SL], DEL[:, :SL], Act.Exp,
                        scale=T["aneg"][:, c * NS + n:c * NS + n + 1]).then_inc(s, i),
                        ["FDL", "aneg"], [f"a_{n}", "A32"])
                def xb_f(e, s, i):
                    e.tensor_tensor(ar[:, o_xs2:o_xs2 + NS * SL].rearrange("p (n t) -> p n t", n=NS),
                                    T["dut"][:].rearrange("p (o t) -> p o t", o=1).broadcast_to([128, NS, SL]),
                                    ar[:, o_BB:o_BB + NS * SL].rearrange("p (n t) -> p n t", n=NS),
                                    Alu.mult).then_inc(s, i)
                vop(xb_f, ["DU"] + bkeys, ["XS2"])
                for n in range(NS):
                    vop(lambda e, s, i, c=c, n=n: e.tensor_tensor_scan(
                        arr(o_xs2 + n * SL, SL), fa[:, f_a32 + n * SL:f_a32 + (n + 1) * SL],
                        arr(o_xs2 + n * SL, SL),
                        T["carry"][:, c * NS + n:c * NS + n + 1], Alu.mult, Alu.add).then_inc(s, i),
                        [f"a_{n}", "XS2", "carry"], [f"h_{n}", "XS2"])
                hkeys = [f"h_{n}" for n in range(NS)]
                def sc_f(e, s, i, c=c):
                    e.tensor_copy(T["carry"][:, c * NS:(c + 1) * NS],
                                  ar[:, o_xs2:o_xs2 + NS * SL].rearrange("p (n t) -> p n t", n=NS)[:, :, SL - 1:SL].rearrange("p n o -> p (n o)")).then_inc(s, i)
                vop(sc_f, hkeys, ["carry"])
                vop(lambda e, s, i: e.tensor_tensor(arr(o_at, NS * SL), arr(o_xs2, NS * SL), arr(o_CB, NS * SL), Alu.mult).then_inc(s, i),
                    hkeys + ckeys, ["ATR"])
                vop(lambda e, s, i: e.tensor_add(arr(o_at, 8 * SL), arr(o_at, 8 * SL), arr(o_at + 8 * SL, 8 * SL)).then_inc(s, i),
                    ["ATR"], ["ATR"])
                vop(lambda e, s, i: e.tensor_add(arr(o_at, 4 * SL), arr(o_at, 4 * SL), arr(o_at + 4 * SL, 4 * SL)).then_inc(s, i),
                    ["ATR"], ["ATR"])
                vop(lambda e, s, i: e.tensor_add(arr(o_at, 2 * SL), arr(o_at, 2 * SL), arr(o_at + 2 * SL, 2 * SL)).then_inc(s, i),
                    ["ATR"], ["ATR"])
                vop(lambda e, s, i: e.tensor_tensor(YC[:, :SL], arr(o_at, SL), arr(o_at + SL, SL), Alu.add).then_inc(s, i),
                    ["ATR"], ["FYC"])
                vop(lambda e, s, i, c=c, so=so: e.scalar_tensor_tensor(
                    YC[:, :SL], arr(u_off(c) + so, SL), T["dpt"][:, c:c + 1],
                    YC[:, :SL], Alu.mult, Alu.add).then_inc(s, i),
                    ["FYC", f"u_{c}", "dpt"], ["FYC"])
                vop(lambda e, s, i, c=c, so=so: e.tensor_tensor(
                    arr(o_res + c * TL + so, SL), YC[:, :SL],
                    arr(o_res + c * TL + so, SL), Alu.mult).then_inc(s, i),
                    ["FYC", f"sr_{c}"], [f"yg_{c}", f"res_{c}", f"sr_{c}"])
        yg_keys = [f"yg_{c}" for c in range(ECH)]

        bank_keys = [f"BNK{m}" for m in range(DCH)]
        for k in range(ECH):
            half = k % 2
            dma(p, "sync", T["wkp"][:, half * D:(half + 1) * D],
                H["wout_bf"][k * 128:(k + 1) * 128, :],
                ["wc_wo"] if ti == 0 and k < 2 else [], [f"wkpH{half}"])
            def mm_out_k(e, sem, inc, k=k, half=half):
                for m in range(DCH):
                    mi = e.matmul(pm[m][:, :], T["wkp"][:, half * D + m * 128:half * D + (m + 1) * 128],
                                  arr(o_res + k * TL, TL), start=(k == 0), stop=(k == ECH - 1),
                                  skip_group_check=True)
                    if m == DCH - 1:
                        mi.then_inc(sem, inc)
            p.op("tensor", mm_out_k, reads=[f"yg_{k}", f"wkpH{half}"], writes=bank_keys)
        for m in range(DCH):
            vop(lambda e, s, i, m=m: e.tensor_copy(arr(o_xs2 + m * TL, TL), pm[m][:, :]).then_inc(s, i),
                [f"BNK{m}"], [f"attb_{m}", "XS2"])
        attb_keys = [f"attb_{m}" for m in range(DCH)]

        ln_stats(o_xs2, o_sq, attb_keys, T["rsb2"], T["nmb2"], "rs2", "nm2")
        for c in range(DCH):
            vop(lambda e, s, i, c=c: e.tensor_tensor(arr(o_sq + c * TL, TL), arr(o_xs2 + c * TL, TL), T["rsb2"][:], Alu.mult).then_inc(s, i),
                attb_keys + ["rs2"], ["SQS"])
            vop(lambda e, s, i, c=c: e.tensor_tensor(arr(o_sq + c * TL, TL), arr(o_sq + c * TL, TL), T["nmb2"][:], Alu.add).then_inc(s, i),
                ["SQS", "nm2"], ["SQS"])
            vop(lambda e, s, i, c=c: e.tensor_tensor(arr(o_at + c * TL, TL), arr(o_sq + c * TL, TL), arr(o_x + c * TL, TL), Alu.add).then_inc(s, i),
                ["SQS", "x", "ATR"], [f"x1b_{c}", "ATR"])
        dma(p, "sync", H["x1d"].rearrange("p (c l) -> p c l", c=DCH)[:, :, t0:t0 + TL],
            arr(o_at, DCH * TL).rearrange("p (c n) -> p c n", c=DCH),
            [f"x1b_{c}" for c in range(DCH)] + ["ATR"], [f"x1d_{ti}"])

    p.barrier()

    for k in range(DCH):
        dma(p, "sync", arr(o_w1 + k * FF, FF), H["w1_bf"][k * 128:(k + 1) * 128, :],
            [], [f"w1r_{k}"])
    w1r_keys = [f"w1r_{k}" for k in range(DCH)]

    for ti in range(NT):
        t0 = ti * TL
        dma(p, "sync", arr(o_x1t, DCH * TL).rearrange("p (c n) -> p c n", c=DCH),
            H["x1d"].rearrange("p (c l) -> p c l", c=DCH)[:, :, t0:t0 + TL],
            [f"x1d_{ti}"], ["x1t"])
        ln_stats(o_x1t, o_sq2, ["x1t"], T["rsb1"], T["nmb1"], "rs3", "nm3")
        for c in range(DCH):
            vop(lambda e, s, i, c=c: e.tensor_tensor(arr(o_sq2 + c * TL, TL), arr(o_x1t + c * TL, TL), T["rsb1"][:], Alu.mult).then_inc(s, i),
                ["x1t", "rs3"], ["SQS"])
            vop(lambda e, s, i, c=c: e.tensor_tensor(arr(o_xn3 + c * TL, TL), arr(o_sq2 + c * TL, TL), T["nmb1"][:], Alu.add).then_inc(s, i),
                ["SQS", "nm3"], [f"xn3_{c}"])
        xn3_keys = [f"xn3_{c}" for c in range(DCH)]

        for f in range(FCH):
            bank = f % 4
            pairs = [(arr(o_w1 + k * FF + f * 128, 128), arr(o_xn3 + k * TL, TL)) for k in range(DCH)]
            mm_group(pm[bank][:, :], pairs, xn3_keys + w1r_keys, [f"BNK{bank}"])
            sop(lambda e, s, i, f=f, bank=bank: e.activation(
                T["fi1"][:], pm[bank][:, :], Act.Identity, bias=T["bb1"][:, f:f + 1],
                scale=cap(C_256)).then_inc(s, i),
                [f"BNK{bank}", "bb1", "cp7"], ["FI1"])
            vop(lambda e, s, i: e.tensor_single_scalar(T["fui"][:], T["fi1"][:], -64, Alu.bitwise_and).then_inc(s, i),
                ["FI1"], ["FUI"])
            vop(lambda e, s, i: e.tensor_single_scalar(T["ffi"][:], T["fi1"][:], 63, Alu.bitwise_and).then_inc(s, i),
                ["FI1"], ["FFI"])
            sop(lambda e, s, i: e.activation(SP1, T["fui"][:], Act.Gelu,
                                             bias=cap(C_M8), scale=cap(C_I256)).then_inc(s, i),
                ["FUI", "cp0", "cp2"], ["FS1"])
            sop(lambda e, s, i: e.activation(SP2, T["fui"][:], Act.Gelu,
                                             bias=cap(C_M7_75), scale=cap(C_I256)).then_inc(s, i),
                ["FUI", "cp1", "cp2"], ["FS2"])
            vop(lambda e, s, i: e.tensor_sub(SP2, SP2, SP1).then_inc(s, i),
                ["FS2", "FS1"], ["FS2"])
            vop(lambda e, s, i: e.tensor_tensor(DEL, T["ffi"][:], SP2, Alu.mult).then_inc(s, i),
                ["FFI", "FS2"], ["FDL"])
            vop(lambda e, s, i: e.scalar_tensor_tensor(YC, DEL, 1.0 / 64, SP1, Alu.mult, Alu.add).then_inc(s, i),
                ["FDL", "FS1"], ["FYC"])
            vop(lambda e, s, i: e.tensor_scalar(T["ffi"][:], YC, 256.0, QB, Alu.mult, Alu.add).then_inc(s, i),
                ["FYC"], ["FFI"])
            vop(lambda e, s, i, f=f: e.tensor_scalar(arr(o_hq + f * TL, TL), T["ffi"][:], 1.0 / 256, -8.0, Alu.mult, Alu.add).then_inc(s, i),
                ["FFI"], [f"hq_{f}"])
        bank_keys = [f"BNK{m}" for m in range(DCH)]
        for k in range(FCH):
            half = k % 2
            dma(p, "sync", T["wkp"][:, half * D:(half + 1) * D],
                H["w2_bf"][k * 128:(k + 1) * 128, :],
                ["wc_w2"] if ti == 0 and k < 2 else [], [f"wkpH{half}"])
            def mm_w2_k(e, sem, inc, k=k, half=half):
                for m in range(DCH):
                    mi = e.matmul(pm[m][:, :], T["wkp"][:, half * D + m * 128:half * D + (m + 1) * 128],
                                  arr(o_hq + k * TL, TL), start=(k == 0), stop=(k == FCH - 1),
                                  skip_group_check=True)
                    if m == DCH - 1:
                        mi.then_inc(sem, inc)
            p.op("tensor", mm_w2_k, reads=[f"hq_{k}", f"wkpH{half}"], writes=bank_keys)
        for m in range(DCH):
            sop(lambda e, s, i, m=m: e.activation(T["fi1"][:], pm[m][:, :], Act.Identity,
                                                  bias=T["bb2"][:, m:m + 1], scale=cap(C_256)).then_inc(s, i),
                [f"BNK{m}", "bb2", "cp7"], ["FI1"])
            vop(lambda e, s, i, m=m: e.tensor_scalar(arr(o_ob + m * TL, TL), T["fi1"][:], 1.0 / 256, -8.0, Alu.mult, Alu.add).then_inc(s, i),
                ["FI1"], [f"ob_{m}"])
        ob_keys = [f"ob_{m}" for m in range(DCH)]

        ln_stats(o_ob, o_sq2, ob_keys, T["rsb2"], T["nmb2"], "rs4", "nm4")
        for c in range(DCH):
            vop(lambda e, s, i, c=c: e.tensor_tensor(arr(o_sq2 + c * TL, TL), arr(o_ob + c * TL, TL), T["rsb2"][:], Alu.mult).then_inc(s, i),
                ob_keys + ["rs4"], ["SQS"])
            vop(lambda e, s, i, c=c: e.tensor_tensor(arr(o_sq2 + c * TL, TL), arr(o_sq2 + c * TL, TL), T["nmb2"][:], Alu.add).then_inc(s, i),
                ["SQS", "nm4"], ["SQS"])
            vop(lambda e, s, i, c=c: e.tensor_tensor(SP1, arr(o_sq2 + c * TL, TL), arr(o_x1t + c * TL, TL), Alu.add).then_inc(s, i),
                ["SQS", "x1t", "FS1"], ["FS1"])
            dma(p, "sync", H["outT"].rearrange("p (cc l) -> p cc l", cc=DCH)[:, c, t0:t0 + TL], SP1,
                ["FS1"], [f"st_{ti}_{c}", "FS1"])

    p.final_waits("sync")


def kernel(L0_attn_x, L0_attn_in_proj_w, L0_attn_conv_w, L0_attn_conv_b,
           L0_attn_x_proj_w, L0_attn_dt_proj_w, L0_attn_dt_proj_b, L0_attn_A_log,
           L0_attn_D, L0_attn_out_proj_w, L0_ffn_w1, L0_ffn_b1, L0_ffn_w2, L0_ffn_b2):
    x = np.asarray(L0_attn_x, dtype=np.float32)
    B = x.shape[0]

    def chunk_major(v, nch):
        return np.ascontiguousarray(np.asarray(v, np.float32).reshape(nch, 128).T)

    wmaps = dict(
        win_T=np.ascontiguousarray(np.asarray(L0_attn_in_proj_w, np.float32).T),
        wxp_T=np.ascontiguousarray(np.asarray(L0_attn_x_proj_w, np.float32).T.reshape(ECH, 128, 96).transpose(1, 0, 2).reshape(128, ECH * 96)),
        wdt_T=np.ascontiguousarray(np.asarray(L0_attn_dt_proj_w, np.float32).T),
        wout_T=np.ascontiguousarray(np.asarray(L0_attn_out_proj_w, np.float32).T),
        w1h=np.ascontiguousarray(np.asarray(L0_ffn_w1, np.float32)),
        w2h=np.ascontiguousarray(np.asarray(L0_ffn_w2, np.float32)),
        convw=np.ascontiguousarray(
            np.asarray(L0_attn_conv_w, np.float32).reshape(E, 4).reshape(ECH, 128, 4).transpose(1, 0, 2).reshape(128, ECH * 4)),
        convb=chunk_major(L0_attn_conv_b, ECH),
        dtbh=chunk_major(L0_attn_dt_proj_b, ECH),
        alog=np.ascontiguousarray(
            np.asarray(L0_attn_A_log, np.float32).reshape(ECH, 128, NS).transpose(1, 0, 2).reshape(128, ECH * NS)),
        dpar=chunk_major(L0_attn_D, ECH),
        b1h=chunk_major(L0_ffn_b1, FCH),
        b2h=chunk_major(L0_ffn_b2, DCH),
    )
    nc = build()
    in_maps = []
    for c in range(8):
        s = c % B
        m = dict(wmaps)
        m["xT"] = np.ascontiguousarray(x[s].T.reshape(DCH, 128, L).transpose(1, 0, 2).reshape(128, DCH * L))
        in_maps.append(m)
    res = run_bass_kernel_spmd(nc, in_maps, core_ids=list(range(8)))
    out = np.empty((B, L, D), np.float32)
    for s in range(B):
        o = res.results[s]["outT"].reshape(128, DCH, L).transpose(1, 0, 2).reshape(D, L)
        out[s] = o.T
    return out
